# version 02
# baseline (speedup 1.0000x reference)
import numpy as np
import ml_dtypes
from contextlib import ExitStack

import concourse.bass as bass
import concourse.mybir as mybir
from concourse.bass_utils import run_bass_kernel_spmd

F32 = mybir.dt.float32
BF16 = mybir.dt.bfloat16
AF = mybir.ActivationFunctionType
ALU = mybir.AluOpType
AX = mybir.AxisListType

NCORES = 8
B, S, D = 2, 4096, 2048
E, H, T, TH = 3 * D, 4 * D, 1024, 512
KD, KE, KH = D // 128, E // 128, H // 128
EPS = 1e-5
C1 = 12582912.0
C2 = 6755399441055744.0
_DBG = ()
_STOP = None


class _StopEmit(Exception):
    pass

ENGS = ("sync", "gpsimd", "vector", "scalar", "tensor")


class Prog:

    def __init__(self):
        self.ops = {g: [] for g in ENGS}
        self.cnt = {}

    def wait(self, eng, sem, target=None):
        t = self.cnt.get(sem, 0) if target is None else target
        if t > 0:
            self.ops[eng].append(("w", sem, t))

    def op(self, eng, fn, incs=(), direct=False):
        if eng in ("tensor", "vector", "scalar") and incs and not direct:
            self.ops[eng].append(("o", fn, ()))
            self.ops[eng].append(("o", lambda e: e.drain(), tuple(incs)))
        else:
            self.ops[eng].append(("o", fn, tuple(incs)))
        for s, k in incs:
            self.cnt[s] = self.cnt.get(s, 0) + k

    def dma(self, eng, out, in_, sem):
        self.op(eng, lambda g, out=out, in_=in_: g.dma_start(out=out, in_=in_), [(sem, 16)])

    def fence(self, eng):
        self.ops[eng].append(("o", lambda e: e.drain(), ()))

    def n(self, sem):
        return self.cnt.get(sem, 0)


def build():
    nc = bass.Bass("TRN2", target_bir_lowering=False, debug=False,
               detect_race_conditions=False)

    d_x = nc.dram_tensor("xT", [D, T], F32, kind="ExternalInput").ap()
    d_wqkv = nc.dram_tensor("wqkvT", [D, E], F32, kind="ExternalInput").ap()
    d_wout = nc.dram_tensor("woutT", [D, D], F32, kind="ExternalInput").ap()
    d_gws = nc.dram_tensor("gw_slc", [D // 8, H], F32, kind="ExternalInput").ap()
    d_vws = nc.dram_tensor("vw_slc", [D // 8, H], F32, kind="ExternalInput").ap()
    d_ows = nc.dram_tensor("ow_slc", [H // 8, D], F32, kind="ExternalInput").ap()
    d_bqkv = nc.dram_tensor("bqkv", [128, KE], F32, kind="ExternalInput").ap()
    d_bout = nc.dram_tensor("bout", [128, KD], F32, kind="ExternalInput").ap()
    d_bg = nc.dram_tensor("bg", [128, KH], F32, kind="ExternalInput").ap()
    d_bv = nc.dram_tensor("bv", [128, KH], F32, kind="ExternalInput").ap()
    d_bo = nc.dram_tensor("bo", [128, KD], F32, kind="ExternalInput").ap()
    d_cos = nc.dram_tensor("cosT", [512, T], F32, kind="ExternalInput").ap()
    d_sin = nc.dram_tensor("sinT", [512, T], F32, kind="ExternalInput").ap()
    d_id = nc.dram_tensor("ident", [128, 128], F32, kind="ExternalInput").ap()
    d_ones = nc.dram_tensor("onesr", [1, 128], F32, kind="ExternalInput").ap()
    d_out = nc.dram_tensor("outT", [D, T], F32, kind="ExternalOutput").ap()
    dbg_shapes = dict(ln1=[D, T], q=[D, T], k=[D, TH], a=[128, KD], xn=[D, T],
                      xdq=[D, T], gam=[128, 16], row=[128, 96], hp=[H, T],
                      trA=[128, T], trB=[128, T], row2=[128, 96], xp=[128, 96])
    d_dbg = {nm: nc.dram_tensor("dbg_" + nm, dbg_shapes[nm], F32, kind="ExternalOutput").ap()
             for nm in _DBG}

    d_spx = nc.dram_tensor("spill_x", [D, T], F32).ap()
    d_spxn = nc.dram_tensor("spill_xn", [D, T], F32).ap()
    d_sph = nc.dram_tensor("spill_h", [H, T], F32).ap()
    d_spq = nc.dram_tensor("spill_q", [H, T], BF16).ap()
    d_gw_in = nc.dram_tensor("gw_ar_in", [1, 4], F32).ap()
    d_gw_out = nc.dram_tensor("gw_ar_out", [1, 4], F32).ap()
    d_a_in = nc.dram_tensor("a_ar_in", [128, KD], F32).ap()
    d_a_out = nc.dram_tensor("a_ar_out", [128, KD], F32).ap()
    d_agg_in = nc.dram_tensor("ag_g_in", [D // 8, H], BF16).ap()
    d_agg_out = nc.dram_tensor("ag_g_out", [D, H], BF16).ap()
    d_agv_in = nc.dram_tensor("ag_v_in", [D // 8, H], BF16).ap()
    d_agv_out = nc.dram_tensor("agvout", [D, H], BF16).ap()
    d_ago_in = nc.dram_tensor("ag_o_in", [H // 8, D], BF16).ap()
    d_ago_out = nc.dram_tensor("ag_o_out", [H, D], BF16).ap()

    with ExitStack() as es:
        e = es.enter_context
        ident = e(nc.sbuf_tensor("identS", [128, 128], F32))
        onesr = e(nc.sbuf_tensor("onesS", [1, 128], F32))
        bqkv = e(nc.sbuf_tensor("bqkvS", [128, KE], F32))
        bout = e(nc.sbuf_tensor("boutS", [128, KD], F32))
        bgc = e(nc.sbuf_tensor("bgS", [128, KH], F32))
        bvc = e(nc.sbuf_tensor("bvS", [128, KH], F32))
        boc = e(nc.sbuf_tensor("boS", [128, KD], F32))
        gcol = e(nc.sbuf_tensor("gcolS", [128, 16], F32))
        trow = e(nc.sbuf_tensor("trowS", [128, 96], F32))
        A = e(nc.sbuf_tensor("aS", [128, KD], F32))
        acol = e(nc.sbuf_tensor("acolS", [128, 2], F32))
        srow = e(nc.sbuf_tensor("srowS", [1, 512], F32))
        bc1 = e(nc.sbuf_tensor("bc1S", [128, T], F32))
        bc2 = e(nc.sbuf_tensor("bc2S", [128, T], F32))
        xwin = e(nc.sbuf_tensor("xwinS", [128, 2 * T], F32))
        scr = e(nc.sbuf_tensor("scrS", [128, 3 * T], F32))
        treeA = e(nc.sbuf_tensor("treeAS", [128, T], F32))
        treeB = e(nc.sbuf_tensor("treeBS", [128, T], F32))
        treeC = e(nc.sbuf_tensor("treeCS", [128, T], F32))
        treeD = e(nc.sbuf_tensor("treeDS", [128, T], F32))
        kbuf = e(nc.sbuf_tensor("kbufS", [128, KD * TH], F32))
        actA = e(nc.sbuf_tensor("actAS", [128, KD * T], BF16))
        actB = e(nc.sbuf_tensor("actBS", [128, KD * T], BF16))
        wwin = e(nc.sbuf_tensor("wwinS", [128, 2 * KD * TH], BF16))
        tabs = e(nc.sbuf_tensor("tabsS", [128, 8 * TH], F32))
        wslc = e(nc.sbuf_tensor("wslcS", [128, T], F32))
        sgn = e(nc.sbuf_tensor("sgnS", [128, 2 * T], BF16))
        junk = e(nc.sbuf_tensor("junkS", [128, TH], F32))

        ps1 = ExitStack()
        pmm = [ps1.enter_context(nc.psum_tensor(f"pmm{i}", [128, TH], F32)) for i in range(4)]
        pt = [ps1.enter_context(nc.psum_tensor(f"pt{i}", [128, 128], F32)) for i in range(2)]
        pbc = ps1.enter_context(nc.psum_tensor("pbc", [128, TH], F32))
        pjk = ps1.enter_context(nc.psum_tensor("pjk", [128, 16], F32))

        P = Prog()

        def stop_after(tag):
            if _STOP == tag:
                raise _StopEmit()

        def TS(out, in0, op0, s1, op1=None, s2=None, accum=None):
            def fn(v, out=out, in0=in0, op0=op0, s1=s1, op1=op1, s2=s2, accum=accum):
                kw = {"accum_out": accum} if accum is not None else {}
                return v.tensor_scalar(out=out, in0=in0, scalar1=s1, scalar2=s2,
                                       op0=op0, op1=(op1 or ALU.bypass), **kw)
            return fn

        def TT(out, in0, in1, op):
            return lambda v, out=out, in0=in0, in1=in1, op=op: \
                v.tensor_tensor(out=out, in0=in0, in1=in1, op=op)

        def STT(out, in0, op0, scl, op1, in1, accum=None):
            def fn(v, out=out, in0=in0, op0=op0, scl=scl, op1=op1, in1=in1, accum=accum):
                kw = {"accum_out": accum} if accum is not None else {}
                return v.scalar_tensor_tensor(out=out, in0=in0, scalar=scl, in1=in1,
                                              op0=op0, op1=op1, **kw)
            return fn

        def CP(out, in_):
            return lambda v, out=out, in_=in_: v.tensor_copy(out=out, in_=in_)

        def RED(out, in_, rop, absf=False):
            return lambda v, out=out, in_=in_, rop=rop, absf=absf: v.tensor_reduce(
                out=out, in_=in_, axis=AX.X, op=rop,
                apply_absolute_value=(True if absf else None))

        def ACT(out, in_, func, bias=None, scale=None, accum=None):
            def fn(s, out=out, in_=in_, func=func, bias=bias, scale=scale, accum=accum):
                kw = {}
                if bias is not None:
                    kw["bias"] = bias
                if scale is not None:
                    kw["scale"] = scale
                if accum is not None:
                    kw["accum_out"] = accum
                return s.activation(out=out, in_=in_, func=func, **kw)
            return fn

        def MM(out, lhsT, rhs, st, sp):
            return lambda t, out=out, l=lhsT, r=rhs, st=st, sp=sp: \
                t.matmul(out, l, r, start=st, stop=sp)

        ceff = gcol[:, 1:2]

        def xw(i):
            return xwin[:, (i % 2) * T:(i % 2) * T + T]

        def xwh(i):
            return xwin[:, (i % 4) * TH:((i % 4) + 1) * TH]

        def scrf(i):
            return scr[:, (i % 3) * T:((i % 3) + 1) * T]

        def scrh(i):
            return scr[:, (i % 6) * TH:((i % 6) + 1) * TH]

        def wtile(buf, k):
            return wwin[:, (buf * KD + k) * TH:(buf * KD + k + 1) * TH]

        def vtile(buf, k):
            return actA[:, (buf * KD + k) * TH:(buf * KD + k + 1) * TH]

        def aA(kc, th=None):
            if th is None:
                return actA[:, kc * T:(kc + 1) * T]
            return actA[:, kc * T + th * TH: kc * T + th * TH + TH]

        def aB(kc, th=None):
            if th is None:
                return actB[:, kc * T:(kc + 1) * T]
            return actB[:, kc * T + th * TH: kc * T + th * TH + TH]

        def kb(kc):
            return kbuf[:, kc * TH:(kc + 1) * TH]

        def kq(i):
            return kbuf[:, (i % 4) * TH:((i % 4) + 1) * TH].bitcast(BF16)

        def owslot(i):
            return tabs[:, (i % 4) * TH:((i % 4) + 1) * TH].bitcast(BF16)

        def qrslot(i):
            return sgn[:, (i % 4) * TH:((i % 4) + 1) * TH]

        onesc = gcol[:, 14:15]
        xps_half = [0]
        xps_free = [0, 0]
        tp_free = []

        def xpart_add(src, grp, nch=8):
            h = xps_half[0]
            xps_half[0] ^= 1
            base = 8 * h
            if xps_free[h] > 0:
                P.wait("tensor", "red", xps_free[h])
            i0 = P.n("xps")
            for tc in range(nch):
                incs = [("xps", 1)] if tc == nch - 1 else []
                P.op("tensor", MM(pjk[:, base + tc:base + tc + 1],
                                  src[:, tc * 128:(tc + 1) * 128], onesc, True, True),
                     incs)
            P.wait("vector", "xps", i0 + 1)
            P.op("vector", CP(trow[:, grp:grp + nch], pjk[:, base:base + nch]),
                 [("red", nch)])
            xps_free[h] = P.n("red")

        def xpart_mm(src, grp, rop, absf=False):
            for tc in range(8):
                i = P.n("tp")
                if i >= 2:
                    P.wait("tensor", "red", tp_free[i - 2])
                P.op("tensor", MM(pt[i % 2][:, :], src[:, tc * 128:(tc + 1) * 128],
                                  ident[:, :], True, True), [("tp", 1)])
                P.wait("vector", "tp", i + 1)
                P.op("vector", RED(trow[:, grp + tc:grp + tc + 1], pt[i % 2][:, :],
                                   rop, absf), [("red", 1)])
                tp_free.append(P.n("red"))

        def xpart(src, grp, rop, absf=False):
            if rop == ALU.add:
                xpart_add(src, grp)
            else:
                xpart_mm(src, grp, rop, absf)

        def bcast(grp, dst, n):
            P.wait("tensor", "rowV", P.n("rowV"))
            for hf in range(n // TH):
                if P.n("bmm") > 0:
                    P.wait("tensor", "bce", P.n("bmm"))
                for tc in range(4):
                    incs = [("tpb", 1)] if tc == 3 else []
                    P.op("tensor", MM(pbc[0:1, tc * 128:(tc + 1) * 128],
                                      trow[:, grp + 4 * hf + tc:grp + 4 * hf + tc + 1],
                                      ident[:, :], True, True), incs)
                P.wait("vector", "tpb", P.n("tpb"))
                P.op("vector", CP(srow[0:1, 0:TH], pbc[0:1, 0:TH]), [("srt", 1)])
                P.wait("tensor", "srt", P.n("srt"))
                P.op("tensor", MM(pbc[:, :], onesr[0:1, :], srow[0:1, 0:TH], True, True),
                     [("bmm", 1)])
                P.wait("vector", "bmm", P.n("bmm"))
                P.op("vector", CP(dst[:, hf * TH:(hf + 1) * TH], pbc[:, :]), [("bce", 1)])

        try:
            for dst, src in [(ident, d_id), (onesr, d_ones), (bqkv, d_bqkv), (bout, d_bout),
                             (bgc, d_bg), (bvc, d_bv), (boc, d_bo)]:
                P.dma("sync", dst[:, :], src[:, :], "cst")

            P.op("vector", lambda v: v.memset(gcol[:, 0:1], 1.0))
            P.fence("vector")
            P.op("vector", TS(gcol[:, 0:1], gcol[:, 0:1], ALU.add, 2.0 ** -30, ALU.subtract, 1.0))
            P.fence("vector")
            P.op("vector", TS(gcol[:, 0:1], gcol[:, 0:1], ALU.mult, 2.0 ** 30))
            P.fence("vector")
            P.op("vector", TS(gcol[:, 1:2], gcol[:, 0:1], ALU.mult, C2 - C1, ALU.add, C1))
            if d_dbg:
                P.op("vector", lambda v: v.memset(trow[:, :], 0.0))
                P.op("vector", lambda v: v.memset(gcol[:, 2:16], 0.0))
                P.fence("vector")
            P.op("vector", lambda v: v.memset(gcol[:, 12:13], EPS))
            P.op("vector", lambda v: v.memset(gcol[:, 14:15], 1.0))
            P.op("vector", lambda v: v.memset(A[:, :], 0.0))
            P.op("vector", lambda v: v.memset(trow[:, 86:90], 0.0))

            stop_after("cst")
            for k in range(KD):
                if k >= 2:
                    P.wait("sync", "tr", 2 * (k - 1))
                    P.wait("sync", "sq1", k - 1)
                P.dma("sync", xw(k), d_x[k * 128:(k + 1) * 128, :], "x1")
            P.dma("sync", junk[:, 0:8], d_x[0:128, 0:8], "x1")
            for k in range(KD):
                P.wait("scalar", "x1", 16 * (k + 2))
                if k >= 2:
                    P.wait("scalar", "tr", 2 * k)
                P.op("scalar", ACT(scrf(k % 2), xw(k), AF.Square), [("sq1", 1)])
            for k in range(KD):
                P.wait("vector", "x1", 16 * (k + 2))
                fn = CP(treeA[:, :], xw(k)) if k == 0 else TT(treeA[:, :], treeA[:, :], xw(k), ALU.add)
                P.op("vector", fn, [("tr", 1)])
                P.wait("vector", "sq1", k + 1)
                fn = CP(treeB[:, :], scrf(0)) if k == 0 else \
                    TT(treeB[:, :], treeB[:, :], scrf(k % 2), ALU.add)
                P.op("vector", fn, [("tr", 1)])
            stop_after("stats")
            P.wait("tensor", "cst", 16)
            P.wait("tensor", "tr", P.n("tr"))
            xpart(treeA, 0, ALU.add)
            xpart(treeB, 8, ALU.add)
            P.wait("vector", "red", P.n("red"))
            P.op("vector", TS(trow[:, 0:8], trow[:, 0:8], ALU.mult, 1.0 / D))
            P.fence("vector")
            P.op("vector", TS(trow[:, 8:16], trow[:, 8:16], ALU.mult, 1.0 / D))
            P.fence("vector")
            P.op("vector", TT(trow[:, 72:80], trow[:, 0:8], trow[:, 0:8], ALU.mult))
            P.fence("vector")
            P.op("vector", TT(trow[:, 8:16], trow[:, 8:16], trow[:, 72:80], ALU.subtract),
                 [("rowV", 1)])
            P.wait("scalar", "rowV", P.n("rowV"))
            P.op("scalar", ACT(trow[:, 72:80], trow[:, 8:16], AF.Sqrt, bias=gcol[:, 12:13]),
                 [("rowS", 1)])
            P.wait("vector", "rowS", P.n("rowS"))
            P.op("vector", lambda v: v.reciprocal(out=trow[:, 8:16], in_=trow[:, 72:80]),
                 [("rowV", 1)])
            stop_after("rows")
            bcast(0, bc1, T)
            bcast(8, bc2, T)
            stop_after("bc")

            for k in range(KD):
                if k >= 2:
                    P.wait("sync", "spx", 16 * (k - 1))
                else:
                    P.wait("sync", "tr", 2 * KD)
                P.dma("sync", xw(k), d_x[k * 128:(k + 1) * 128, :], "x2")
                if k >= 1:
                    P.wait("sync", "fold", k)
                    P.dma("sync", d_spx[(k - 1) * 128:k * 128, :], xw(k - 1), "spx")
            P.dma("sync", junk[:, 0:8], d_x[0:128, 0:8], "x2")
            P.wait("sync", "fold", KD)
            P.dma("sync", d_spx[(KD - 1) * 128:KD * 128, :], xw(KD - 1), "spx")
            P.wait("vector", "bce", P.n("bce"))
            for k in range(KD):
                P.wait("vector", "x2", 16 * (k + 2))
                P.op("vector", TT(scrf(2), xw(k), bc1[:, :], ALU.subtract))
                P.op("vector", TT(aA(k), scrf(2), bc2[:, :], ALU.mult), [("ln", 1)])
                P.op("vector", TT(xw(k), xw(k), aA(k), ALU.add), [("fold", 1)])

            for cc in range(4):
                P.dma("sync", tabs[:, cc * TH:(cc + 1) * TH],
                      d_cos[cc * 128:(cc + 1) * 128, 0:TH], "tab")
            for cc in range(4):
                P.dma("sync", tabs[:, (4 + cc) * TH:(5 + cc) * TH],
                      d_sin[cc * 128:(cc + 1) * 128, 0:TH], "tab")

            slabs = []
            for mi, (dsl, nrow) in enumerate(((d_gws, 2), (d_vws, 2), (d_ows, 8))):
                ncol = (H if nrow == 2 else D) // TH
                for r in range(nrow):
                    for c in range(ncol):
                        slabs.append((mi, dsl, r, c))
            assert len(slabs) == 96
            _a0 = _STOP in ("a0k", "a0kr", "a0q", "a0n", "attn0")
            for i, (mi, dsl, r, c) in enumerate(slabs):
                if i >= 2 and not _a0:
                    P.wait("sync", "gab", i - 1)
                P.dma("sync", wslc[:, (i % 2) * TH:(i % 2 + 1) * TH],
                      dsl[r * 128:(r + 1) * 128, c * TH:(c + 1) * TH], "slc")
            P.dma("sync", junk[:, 0:8], d_gws[0:128, 0:8], "slc")

            sgn32 = sgn[:, 0:T].bitcast(F32)
            WIN_DONE = {}

            def attn_half(th, te_hook=None, dve_hook=None, scalar_hook=None,
                          qevac_extra=None, vstt_extra=None):
                if th == 1:
                    P.wait("sync", "rpQ", P.n("rpQ"))
                    for cc in range(4):
                        P.dma("sync", tabs[:, cc * TH:(cc + 1) * TH],
                              d_cos[cc * 128:(cc + 1) * 128, TH:T], "tab")
                    for cc in range(4):
                        P.dma("sync", tabs[:, (4 + cc) * TH:(5 + cc) * TH],
                              d_sin[cc * 128:(cc + 1) * 128, TH:T], "tab")
                tab_tgt = P.n("tab")

                def cosT(cc):
                    return tabs[:, cc * TH:(cc + 1) * TH]

                def sinT(cc):
                    return tabs[:, (4 + cc) * TH:(5 + cc) * TH]

                ph = {"Q": True, "ropeK": True, "ropeQ": True, "norm": True, "V": True}
                if th == 0 and _STOP in ("a0k", "a0kr", "a0q", "a0n"):
                    ph["Q"] = _STOP in ("a0q", "a0n")
                    ph["ropeK"] = _STOP in ("a0kr", "a0q", "a0n")
                    ph["ropeQ"] = False
                    ph["norm"] = _STOP == "a0n"
                    ph["V"] = False
                nwin = 12 if ph["V"] else (8 if ph["Q"] else 4)
                gw0 = P.n("wq") // (16 * KD)
                for wi in range(nwin):
                    g = gw0 + wi
                    if g >= 2:
                        P.wait("gpsimd", "mma", 4 * (g - 1))
                    e0 = [D, 0, 2 * D][wi // 4] + (wi % 4) * TH
                    for k in range(KD):
                        P.dma("gpsimd", wtile(g % 2, k),
                              d_wqkv[k * 128:(k + 1) * 128, e0:e0 + TH], "wq")

                rhs = [aA(k, th) for k in range(KD)]
                mma0 = P.n("mma")
                evA0 = P.n("evA")

                def mm_group(j):
                    g = gw0 + j // 4
                    bank = P.n("mma") % 4
                    P.wait("tensor", "wq", 16 * KD * (g + 1))
                    if P.n("mma") >= 4:
                        P.wait("tensor", "evA", P.n("mma") - 3)
                    for k in range(KD):
                        lt = wtile(g % 2, k)[:, (j % 4) * 128:(j % 4) * 128 + 128]
                        incs = [("mma", 1)] if k == KD - 1 else []
                        P.op("tensor", MM(pmm[bank][:, :], lt, rhs[k], k == 0, k == KD - 1), incs)
                    if j % 4 == 3:
                        WIN_DONE[g] = ("mma", P.n("mma"))
                    return bank

                P.wait("tensor", "ln", KD)
                kbank = [mm_group(j) for j in range(16)]
                if te_hook:
                    te_hook()
                qbank = [mm_group(j) for j in range(16, 32)] if ph["Q"] else []

                for j in range(16):
                    P.wait("vector", "mma", mma0 + j + 1)
                    P.op("vector", TS(kb(j), pmm[kbank[j]][:, :], ALU.add, bqkv[:, 16 + j:17 + j]),
                         [("evA", 1)])
                if dve_hook:
                    dve_hook()
                for j in range(16 if ph["Q"] else 0):
                    P.wait("vector", "mma", mma0 + 16 + j + 1)
                    P.op("vector", TS(aB(j, th), pmm[qbank[j]][:, :], ALU.add, bqkv[:, j:j + 1]),
                         [("evA", 1)])
                    if qevac_extra:
                        qevac_extra(j)

                P.wait("vector", "tab", tab_tgt)

                def rope(pair, last_sem):
                    for ec in range(4):
                        ev, od = pair(ec)
                        P.op("vector", TT(scrh(0), ev, cosT(ec), ALU.mult))
                        P.op("vector", TT(scrh(1), ev, sinT(ec), ALU.mult))
                        P.op("vector", TT(scrh(2), od, sinT(ec), ALU.mult))
                        P.op("vector", TT(scrh(3), od, cosT(ec), ALU.mult))
                        P.op("vector", TT(ev, scrh(0), scrh(2), ALU.subtract))
                        P.op("vector", TT(od, scrh(3), scrh(1), ALU.add),
                             [(last_sem, 1)] if ec == 3 else ())

                if ph["ropeK"]:
                    rope(lambda ec: (kb(ec), kb(ec + 4)), "rpK")
                if ph["ropeQ"]:
                    rope(lambda ec: (aB(ec, th), aB(ec + 4, th)), "rpQ")

                if scalar_hook:
                    scalar_hook()
                if not ph["norm"] and th == 0 and _STOP in ("a0k", "a0kr", "a0q"):
                    raise _StopEmit()
                sqK0 = P.n("sqK")
                trK0 = P.n("trK")
                for kc in range(KD):
                    if kc < 8:
                        P.wait("scalar", "rpK", P.n("rpK"))
                    else:
                        P.wait("scalar", "evA", evA0 + kc + 1)
                    if kc >= 2:
                        P.wait("scalar", "trK", trK0 + kc - 1)
                    P.op("scalar", ACT(scrh(4 + kc % 2), kb(kc), AF.Square), [("sqK", 1)])
                for kc in range(KD):
                    P.wait("vector", "sqK", sqK0 + kc + 1)
                    fn = CP(treeA[:, 0:TH], scrh(4)) if kc == 0 else \
                        TT(treeA[:, 0:TH], treeA[:, 0:TH], scrh(4 + kc % 2), ALU.add)
                    P.op("vector", fn, [("trK", 1)])
                P.wait("tensor", "trK", trK0 + KD)
                xpart_add(treeA, 72, nch=4)
                P.wait("scalar", "red", P.n("red"))
                P.op("scalar", ACT(trow[:, 76:80], trow[:, 72:76], AF.Sqrt), [("rowS", 1)])
                P.wait("vector", "rowS", P.n("rowS"))
                P.op("vector", TS(trow[:, 76:80], trow[:, 76:80], ALU.max, 1e-12), [("rowV", 1)])
                P.op("vector", lambda v: v.reciprocal(out=trow[:, 80:84], in_=trow[:, 76:80]),
                     [("rowV", 1)])
                bcast(80, bc1, TH)

                P.wait("vector", "bce", P.n("bce"))
                for kc in range(KD):
                    P.op("vector", TT(kb(kc), kb(kc), bc1[:, 0:TH], ALU.mult))

                if not ph["V"] and th == 0 and _STOP == "a0n":
                    raise _StopEmit()
                vbank = [mm_group(j) for j in range(32, 48)]

                for j in range(16):
                    P.wait("vector", "mma", mma0 + 32 + j + 1)
                    P.op("vector", STT(junk[:, :], pmm[vbank[j]][:, :], ALU.add,
                                       bqkv[:, 32 + j:33 + j], ALU.mult, kb(j),
                                       accum=acol[:, 0:1]), [("evA", 1)])
                    P.op("vector", TT(A[:, j:j + 1], A[:, j:j + 1], acol[:, 0:1], ALU.add),
                         [("aAcc", 1)])
                    if vstt_extra:
                        vstt_extra(j)

            attn_half(0)
            stop_after("attn0")

            for i, (mi, dsl, r, c) in enumerate(slabs):
                P.wait("scalar", "slc", 16 * (i + 2))
                if i >= 2:
                    P.wait("scalar", "gacD", i - 1)
                P.op("scalar", ACT(sgn32, wslc[:, (i % 2) * TH:(i % 2 + 1) * TH], AF.Abs,
                                   accum=trow[:, 84 + i % 2:85 + i % 2]), [("gab", 1)])
                P.wait("vector", "gab", P.n("gab"))
                P.op("vector", TT(trow[:, 86 + mi:87 + mi], trow[:, 86 + mi:87 + mi],
                                  trow[:, 84 + i % 2:85 + i % 2], ALU.add), [("gacD", 1)])
            P.wait("tensor", "gacD", 96)
            h = xps_half[0]
            xps_half[0] ^= 1
            if xps_free[h] > 0:
                P.wait("tensor", "red", xps_free[h])
            P.op("tensor", MM(pjk[0:4, 8 * h:8 * h + 1], trow[:, 86:90], onesc,
                              True, True), [("xps", 1)])
            P.wait("vector", "xps", P.n("xps"))
            P.op("vector", CP(trow[0:4, 94:95], pjk[0:4, 8 * h:8 * h + 1]), [("red", 1)])
            xps_free[h] = P.n("red")
            P.wait("gpsimd", "red", P.n("red"))
            P.dma("gpsimd", d_gw_in[0:1, 0:4], trow[0:4, 94:95], "ccd")
            P.wait("gpsimd", "ccd", P.n("ccd"))
            P.op("gpsimd", lambda g: g.collective_compute(
                "AllReduce", ALU.add, replica_groups=[list(range(8))],
                ins=[d_gw_in[:, :].opt()], outs=[d_gw_out[:, :].opt()]), [("cc", 1)])
            P.wait("gpsimd", "cc", P.n("cc"))
            P.dma("gpsimd", trow[0:1, 90:94], d_gw_out[0:1, 0:4], "ccd")
            P.dma("gpsimd", junk[0:1, 0:4], d_gw_out[0:1, 0:4], "ccd")
            gw_ready = P.n("ccd")

            agd = [d_agg_in, d_agv_in, d_ago_in]
            psl = sgn[:, 0:TH]

            def co(i):
                return sgn[:, (1 + i % 2) * TH:(2 + i % 2) * TH]

            def th_te_hook():
                P.wait("tensor", "ccd", gw_ready)
                for mi in range(3):
                    if P.n("bmm") > 0:
                        P.wait("tensor", "bce", P.n("bmm"))
                    P.op("tensor", MM(pbc[:, 0:2], onesr[0:1, :], trow[0:1, 90 + mi:92 + mi],
                                      True, True), [("bmm", 1)])

            def th_dve_hook():
                for mi in range(3):
                    P.wait("vector", "bmm", P.n("bmm") - 2 + mi)
                    P.op("vector", TS(gcol[:, 13:14], pbc[:, 0:1], ALU.mult,
                                      1.0 / (D * H), ALU.max, 1e-5), [("bce", 1)])
                    P.op("vector", TS(gcol[:, 2 + 2 * mi:3 + 2 * mi], gcol[:, 13:14],
                                      ALU.mult, 0.5))
                    P.op("vector", TS(gcol[:, 3 + 2 * mi:4 + 2 * mi], gcol[:, 13:14],
                                      ALU.mult, -0.5))
                    P.op("vector", TS(gcol[:, 8 + mi:9 + mi], gcol[:, 13:14],
                                      ALU.mult, -1.0 / 127.0), [("thr", 1)])

            trn_i = [0]

            def emit_tern(nmax):
                while trn_i[0] < min(nmax, 96):
                    i = trn_i[0]
                    mi = slabs[i][0]
                    wsl = wslc[:, (i % 2) * TH:(i % 2 + 1) * TH]
                    P.wait("vector", "sl2", 16 * (i + 2))
                    if i >= 2:
                        P.wait("vector", "agi", 16 * (i - 1))
                    P.op("vector", TS(psl, wsl, ALU.is_gt, gcol[:, 2 + 2 * mi:3 + 2 * mi]))
                    P.op("vector", STT(co(i), wsl, ALU.is_lt,
                                       gcol[:, 3 + 2 * mi:4 + 2 * mi], ALU.subtract, psl),
                         [("trn", 1)])
                    trn_i[0] += 1

            if _STOP == "attn1":
                attn_half(1, te_hook=th_te_hook, dve_hook=th_dve_hook)
            else:
                attn_half(1, te_hook=th_te_hook, dve_hook=th_dve_hook,
                          qevac_extra=lambda j: emit_tern(j + 1),
                          vstt_extra=lambda j: emit_tern(16 + 3 * (j + 1)))
                emit_tern(96)
            qkv_groups_done = P.n("mma")
            stop_after("attn1")

            def agspill(i):
                mi, dsl, r, c = slabs[i]
                P.wait("sync", "trn", i + 1)
                P.dma("sync", agd[mi][r * 128:(r + 1) * 128, c * TH:(c + 1) * TH],
                      co(i), "agi")

            for i, (mi, dsl, r, c) in enumerate(slabs):
                if i == 0:
                    P.wait("sync", "gab", 96)
                if i >= 2:
                    P.wait("sync", "trn", i - 1)
                P.dma("sync", wslc[:, (i % 2) * TH:(i % 2 + 1) * TH],
                      dsl[r * 128:(r + 1) * 128, c * TH:(c + 1) * TH], "sl2")
                if i >= 1:
                    agspill(i - 1)
            P.dma("sync", junk[:, 0:8], d_gws[0:128, 0:8], "sl2")
            agspill(95)

            def wout_win(w):
                g = 24 + w
                if g - 2 < 24:
                    dsem, dcnt = WIN_DONE[g - 2]
                else:
                    dsem, dcnt = "mmo", 8 * (g - 2 - 24 + 1)
                P.wait("gpsimd", dsem, dcnt)
                for k in range(KD):
                    P.dma("gpsimd", wtile(g % 2, k),
                          d_wout[k * 128:(k + 1) * 128, w * TH:(w + 1) * TH], "wq")

            wout_win(0)
            wout_win(1)
            P.wait("gpsimd", "aAcc", 32)
            P.dma("gpsimd", d_a_in[:, :], A[:, :], "ccd")
            P.wait("gpsimd", "ccd", P.n("ccd"))
            P.op("gpsimd", lambda g: g.collective_compute(
                "AllReduce", ALU.add, replica_groups=[[0, 1, 2, 3], [4, 5, 6, 7]],
                ins=[d_a_in[:, :].opt()], outs=[d_a_out[:, :].opt()]), [("cc", 1)])
            P.wait("gpsimd", "cc", P.n("cc"))
            P.dma("gpsimd", A[:, :], d_a_out[:, :], "ccd")
            P.dma("gpsimd", junk[0:1, 0:4], d_a_out[0:1, 0:4], "ccd")
            a_ready = P.n("ccd")
            def xn_spill(lo, hi):
                for g2s in range(lo, hi):
                    ws, ecls, ths = g2s // 8, (g2s % 8) // 2, g2s % 2
                    ecs = ws * 4 + ecls
                    P.wait("gpsimd", "evO", g2s + 1)
                    P.op("gpsimd", lambda g, ecs=ecs, ths=ths, g2s=g2s: g.dma_start(
                        out=d_spxn[ecs * 128:(ecs + 1) * 128, ths * TH:(ths + 1) * TH],
                        in_=scrh(g2s % 2)), [("spxn", 16)])

            xn_spill(0, 8)
            wout_win(2)
            xn_spill(8, 16)
            wout_win(3)
            xn_spill(16, 32)
            P.wait("gpsimd", "agi", 96 * 16)
            for din, dout in ((d_agg_in, d_agg_out), (d_agv_in, d_agv_out),
                              (d_ago_in, d_ago_out)):
                P.op("gpsimd", lambda g, din=din, dout=dout: g.collective_compute(
                    "AllGather", ALU.bypass, replica_groups=[list(range(8))],
                    ins=[din[:, :].opt()], outs=[dout[:, :].opt()]), [("ag", 1)])
            ag_ready = P.n("ag")

            stop_after("ar")
            P.wait("vector", "ccd", a_ready)
            P.wait("vector", "mma", qkv_groups_done)
            for kc in range(KD):
                P.op("vector", TS(aA(kc), aB(kc), ALU.mult, A[:, kc:kc + 1]), [("y", 1)])

            P.wait("tensor", "y", KD)
            mmo_banks = []
            for g2 in range(32):
                w, ecl, th = g2 // 8, (g2 % 8) // 2, g2 % 2
                gwin = 24 + w
                bank = P.n("mmo") % 4
                mmo_banks.append(bank)
                P.wait("tensor", "wq", 16 * KD * (gwin + 1))
                if g2 >= 4:
                    P.wait("tensor", "evO", g2 - 3)
                if g2 == 0:
                    P.wait("tensor", "evA", P.n("evA"))
                for k in range(KD):
                    lt = wtile(gwin % 2, k)[:, ecl * 128:ecl * 128 + 128]
                    incs = [("mmo", 1)] if k == KD - 1 else []
                    P.op("tensor", MM(pmm[bank][:, :], lt, aA(k, th), k == 0, k == KD - 1), incs)
                if ecl == 3 and th == 1:
                    WIN_DONE[gwin] = ("mmo", P.n("mmo"))
            P.wait("sync", "spx", 256)
            for g2 in range(32):
                w, ecl, th = g2 // 8, (g2 % 8) // 2, g2 % 2
                ec = w * 4 + ecl
                if g2 >= 4:
                    P.wait("sync", "evO", g2 - 3)
                P.dma("sync", xwh(g2), d_spx[ec * 128:(ec + 1) * 128, th * TH:(th + 1) * TH],
                      "xrl")
            for _f in range(3):
                P.dma("sync", junk[:, 0:8], d_spx[0:128, 0:8], "xrl")

            for g2 in range(32):
                w, ecl, th = g2 // 8, (g2 % 8) // 2, g2 % 2
                ec = w * 4 + ecl
                xns = scrh(g2 % 2)
                sqs = scrh(2 + g2 % 2)
                P.wait("vector", "mmo", g2 + 1)
                P.wait("vector", "xrl", 16 * (g2 + 4))
                if g2 >= 2:
                    P.wait("vector", "spxn", 16 * (g2 - 1))
                    P.wait("vector", "sq2", g2 - 1)
                P.op("vector", STT(xns, pmm[mmo_banks[g2]][:, :], ALU.add, bout[:, ec:ec + 1],
                                   ALU.add, xwh(g2)), [("evO", 1)])
                tsl = slice(th * TH, (th + 1) * TH)
                fn = CP(treeA[:, tsl], xns) if ec == 0 else \
                    TT(treeA[:, tsl], treeA[:, tsl], xns, ALU.add)
                P.op("vector", fn, [("t2t", 1)])
                P.wait("scalar", "evO", g2 + 1)
                if g2 >= 2:
                    P.wait("scalar", "t2t", 4 * (g2 - 1) + 2)
                P.op("scalar", ACT(sqs, xns, AF.Square), [("sq2", 1)])
                P.wait("vector", "sq2", P.n("sq2"))
                fn = CP(treeB[:, tsl], sqs) if ec == 0 else \
                    TT(treeB[:, tsl], treeB[:, tsl], sqs, ALU.add)
                P.op("vector", fn, [("t2t", 1)])
                fn = CP(treeC[:, tsl], xns) if ec == 0 else \
                    TT(treeC[:, tsl], treeC[:, tsl], xns, ALU.max)
                P.op("vector", fn, [("t2t", 1)])
                fn = CP(treeD[:, tsl], xns) if ec == 0 else \
                    TT(treeD[:, tsl], treeD[:, tsl], xns, ALU.min)
                P.op("vector", fn, [("t2t", 1)])

            stop_after("outproj")
            if "trA" in d_dbg:
                P.wait("gpsimd", "t2t", 128)
                P.dma("gpsimd", d_dbg["trA"][:, :], treeA[:, :], "dbgd")
                P.dma("gpsimd", d_dbg["trB"][:, :], treeB[:, :], "dbgd")
            P.wait("tensor", "t2t", 128)
            xpart(treeA, 16, ALU.add)
            xpart(treeB, 24, ALU.add)
            xpart(treeC, 48, ALU.max)
            xpart(treeD, 56, ALU.min)
            if "xp" in d_dbg:
                P.wait("gpsimd", "red", P.n("red"))
                P.dma("gpsimd", d_dbg["xp"][:, :], trow[:, :], "dbgd")
            P.wait("vector", "red", P.n("red"))
            P.op("vector", TS(trow[:, 16:24], trow[:, 16:24], ALU.mult, 1.0 / D))
            P.fence("vector")
            P.op("vector", TS(trow[:, 24:32], trow[:, 24:32], ALU.mult, 1.0 / D))
            P.fence("vector")
            P.op("vector", TT(trow[:, 72:80], trow[:, 16:24], trow[:, 16:24], ALU.mult))
            P.fence("vector")
            P.op("vector", TT(trow[:, 24:32], trow[:, 24:32], trow[:, 72:80], ALU.subtract),
                 [("rowV", 1)])
            P.wait("scalar", "rowV", P.n("rowV"))
            P.op("scalar", ACT(trow[:, 72:80], trow[:, 24:32], AF.Sqrt, bias=gcol[:, 12:13]),
                 [("rowS", 1)])
            P.wait("vector", "rowS", P.n("rowS"))
            P.op("vector", lambda v: v.reciprocal(out=trow[:, 24:32], in_=trow[:, 72:80]))
            P.fence("vector")
            P.op("vector", TT(trow[:, 72:80], trow[:, 48:56], trow[:, 16:24], ALU.subtract))
            P.fence("vector")
            P.op("vector", TT(trow[:, 84:92], trow[:, 16:24], trow[:, 56:64], ALU.subtract))
            P.fence("vector")
            P.op("vector", TT(trow[:, 48:56], trow[:, 72:80], trow[:, 84:92], ALU.max))
            P.fence("vector")
            P.op("vector", TT(trow[:, 48:56], trow[:, 24:32], trow[:, 48:56], ALU.mult))
            P.fence("vector")
            P.op("vector", TS(trow[:, 48:56], trow[:, 48:56], ALU.max, 1e-5))
            P.fence("vector")
            P.op("vector", lambda v: v.reciprocal(out=trow[:, 72:80], in_=trow[:, 48:56]))
            P.fence("vector")
            P.op("vector", TT(trow[:, 32:40], trow[:, 24:32], trow[:, 72:80], ALU.mult))
            P.fence("vector")
            P.op("vector", TS(trow[:, 32:40], trow[:, 32:40], ALU.mult, 127.0))
            P.fence("vector")
            P.op("vector", TT(trow[:, 40:48], trow[:, 16:24], trow[:, 32:40], ALU.mult),
                 [("rowV", 1)])
            if "row2" in d_dbg:
                P.wait("gpsimd", "rowV", P.n("rowV"))
                P.dma("gpsimd", d_dbg["row2"][:, :], trow[:, :], "dbgd")
            bcast(32, bc1, T)
            bcast(40, bc2, T)
            P.wait("sync", "spxn", 512)
            P.wait("sync", "evO", 32)
            for kc in range(KD):
                if kc >= 2:
                    P.wait("sync", "xdq", kc - 1)
                P.dma("sync", xw(kc), d_spxn[kc * 128:(kc + 1) * 128, :], "xn1")
            P.dma("sync", junk[:, 0:8], d_spxn[0:128, 0:8], "xn1")
            P.wait("vector", "bce", P.n("bce"))
            for kc in range(KD):
                P.wait("vector", "xn1", 16 * (kc + 2))
                P.op("vector", TT(scrf(2), xw(kc), bc1[:, :], ALU.mult))
                P.op("vector", TT(scrf(2), scrf(2), bc2[:, :], ALU.subtract))
                P.op("vector", TS(aB(kc), scrf(2), ALU.add, ceff, ALU.subtract, ceff),
                     [("xdq", 1)])
            bcast(48, bc1, T)

            stop_after("xdq")
            P.wait("gpsimd", "mmo", 32)
            P.wait("gpsimd", "ag", ag_ready)

            def gvwin(wgi):
                if wgi >= 2:
                    P.wait("gpsimd", "mmf", 16 * (wgi - 1))
                for k in range(KD):
                    P.dma("gpsimd", wtile(wgi % 2, k),
                          d_agg_out[k * 128:(k + 1) * 128, wgi * TH:(wgi + 1) * TH], "gwW")
                for k in range(KD):
                    P.dma("gpsimd", vtile(wgi % 2, k),
                          d_agv_out[k * 128:(k + 1) * 128, wgi * TH:(wgi + 1) * TH], "vwW")

            gvwin(0)
            gvwin(1)

            P.wait("tensor", "xdq", KD)
            P.wait("tensor", "evO", 32)
            for hc in range(KH):
                wgi = hc // 4
                hl = hc % 4
                if hc % 4 == 3 and hc // 4 + 2 < 16:
                    gvwin(hc // 4 + 2)
                P.wait("tensor", "gwW", 16 * KD * (wgi + 1))
                P.wait("tensor", "vwW", 16 * KD * (wgi + 1))
                for mat in range(2):
                    for th in range(2):
                        gi = P.n("mmf")
                        bank = 2 * mat + th
                        if gi >= 4:
                            P.wait("tensor", "evF", gi - 3)
                        tile = wtile if mat == 0 else vtile
                        for k in range(KD):
                            lt = tile(wgi % 2, k)[:, hl * 128:hl * 128 + 128]
                            incs = [("mmf", 1)] if k == KD - 1 else []
                            P.op("tensor", MM(pmm[bank][:, :], lt, aB(k, th), k == 0, k == KD - 1),
                                 incs)
                for th in range(2):
                    gidx = 2 * hc + th
                    P.wait("vector", "mmf", 4 * hc + th + 1)
                    if hc >= 1:
                        P.wait("vector", "sil", 2 * (hc - 1) + th + 1)
                    P.op("vector", STT(scrh(th), pmm[th][:, :], ALU.mult, gcol[:, 8:9],
                                       ALU.mult, bc1[:, th * TH:(th + 1) * TH]), [("evF", 1)])
                for th in range(2):
                    P.wait("scalar", "evF", 4 * hc + th + 1)
                    if hc >= 1:
                        P.wait("scalar", "hpw", 2 * (hc - 1) + th + 1)
                    P.op("scalar", ACT(scrh(2 + th), scrh(th), AF.Silu, bias=bgc[:, hc:hc + 1]),
                         [("sil", 1)])
                for th in range(2):
                    P.wait("vector", "mmf", 4 * hc + 2 + th + 1)
                    P.op("vector", STT(scrh(4 + th), pmm[2 + th][:, :], ALU.mult, gcol[:, 9:10],
                                       ALU.mult, bc1[:, th * TH:(th + 1) * TH]), [("evF", 1)])
                    P.wait("vector", "sil", 2 * hc + th + 1)
                    hps = kb(2 * (hc % 4) + th)
                    if hc >= 4:
                        P.wait("vector", "sph", 32 * (hc - 3))
                    P.op("vector", STT(hps, scrh(4 + th), ALU.add, bvc[:, hc:hc + 1],
                                       ALU.mult, scrh(2 + th)), [("hpw", 1)])
                    P.wait("gpsimd", "hpw", 2 * hc + th + 1)
                    P.op("gpsimd", lambda g, hc=hc, th=th, hps=hps: g.dma_start(
                        out=d_sph[hc * 128:(hc + 1) * 128, th * TH:(th + 1) * TH], in_=hps),
                        [("sph", 16)])
                    tsl = slice(th * TH, (th + 1) * TH)
                    fn = CP(treeC[:, tsl], hps) if hc == 0 else \
                        TT(treeC[:, tsl], treeC[:, tsl], hps, ALU.max)
                    P.op("vector", fn, [("hpt", 1)])
                    fn = CP(treeD[:, tsl], hps) if hc == 0 else \
                        TT(treeD[:, tsl], treeD[:, tsl], hps, ALU.min)
                    P.op("vector", fn, [("hpt", 1)])

            stop_after("gv")
            P.wait("tensor", "hpt", 4 * KH)
            xpart(treeC, 56, ALU.max)
            xpart(treeD, 72, ALU.min)
            P.wait("vector", "red", P.n("red"))
            P.op("vector", TS(trow[:, 72:80], trow[:, 72:80], ALU.mult, -1.0))
            P.fence("vector")
            P.op("vector", TT(trow[:, 56:64], trow[:, 56:64], trow[:, 72:80], ALU.max))
            P.fence("vector")
            P.op("vector", TS(trow[:, 56:64], trow[:, 56:64], ALU.max, 1e-5))
            P.fence("vector")
            P.op("vector", lambda v: v.reciprocal(out=trow[:, 72:80], in_=trow[:, 56:64]))
            P.fence("vector")
            P.op("vector", TS(trow[:, 64:72], trow[:, 72:80], ALU.mult, 127.0), [("rowV", 1)])
            bcast(64, bc2, T)
            bcast(56, bc1, T)

            P.wait("sync", "sph", 32 * KH)
            for kcq in range(KH):
                if kcq >= 2:
                    P.wait("sync", "qz", kcq - 1)
                tgt = treeA if kcq % 2 == 0 else treeB
                P.dma("sync", tgt[:, :], d_sph[kcq * 128:(kcq + 1) * 128, :], "hrd")
            P.dma("sync", junk[:, 0:8], d_sph[0:128, 0:8], "hrd")
            P.wait("vector", "bce", P.n("bce"))
            for kcq in range(KH):
                P.wait("vector", "hrd", 16 * (kcq + 2))
                tgt = treeA if kcq % 2 == 0 else treeB
                P.op("vector", TT(scrf(0), tgt[:, :], bc2[:, :], ALU.mult))
                if kcq >= 4:
                    P.wait("vector", "spq", 16 * (kcq - 3))
                P.op("vector", TS(kq(kcq), scrf(0), ALU.add, ceff, ALU.subtract, ceff),
                     [("qz", 1)])


            for q in range(KH):
                P.wait("gpsimd", "qz", q + 1)
                P.op("gpsimd", lambda g, q=q: g.dma_start(
                    out=d_spq[q * 128:(q + 1) * 128, :], in_=kq(q)), [("spq", 16)])
            stop_after("quant")
            bce_all = P.n("bce")
            evF_all = P.n("evF")
            ps1.close()
            pm8 = [es.enter_context(nc.psum_tensor(f"pm8_{i}", [128, TH], F32)) for i in range(8)]

            def ow_pass(p):
                dg = p // 2
                sem = "owA" if p % 2 == 0 else "owB"
                for hc in range(KH):
                    idx = p * KH + hc
                    if idx >= 4:
                        P.wait("gpsimd", "mmp", idx - 3)
                    P.dma("gpsimd", owslot(idx),
                          d_ago_out[hc * 128:(hc + 1) * 128, dg * 1024:(dg + 1) * 1024], sem)
                for _f in range(3):
                    P.dma("gpsimd", junk[0:1, 0:8].bitcast(BF16), d_ago_out[0:1, 0:16], sem)
            P.wait("sync", "xdq", KD)

            def qrd_pass(p):
                dg, th = p // 2, p % 2
                sem = "qrdA" if p % 2 == 0 else "qrdB"
                for hc in range(KH):
                    idx = p * KH + hc
                    if hc == 0:
                        P.wait("sync", "spq", 16 * KH)
                    if idx >= 4:
                        P.wait("sync", "mmp", idx - 3)
                    P.dma("sync", qrslot(idx),
                          d_spq[hc * 128:(hc + 1) * 128, th * TH:(th + 1) * TH], sem)
                for _f in range(3):
                    P.dma("sync", junk[0:1, 0:8].bitcast(BF16), d_spq[0:1, 0:16], sem)

            P.wait("tensor", "bce", bce_all)
            P.wait("tensor", "evF", evF_all)
            P.wait("tensor", "agi", 96 * 16)
            for p in range(4):
                dg, th = p // 2, p % 2
                ow_pass(p)
                qrd_pass(p)
                if p >= 1:
                    P.wait("tensor", "evP", 8 * p)
                owsem = "owA" if p % 2 == 0 else "owB"
                qrsem = "qrdA" if p % 2 == 0 else "qrdB"
                pbase = (p // 2) * (KH + 3) * 16
                for hc in range(KH):
                    idx = p * KH + hc
                    P.wait("tensor", owsem, pbase + 16 * min(hc + 4, KH + 3))
                    P.wait("tensor", qrsem, pbase + 16 * min(hc + 4, KH + 3))
                    for d in range(8):
                        incs = [("mmp", 1)] if d == 7 else []
                        P.op("tensor", MM(pm8[d][:, :], owslot(idx)[:, d * 128:(d + 1) * 128],
                                          qrslot(idx), hc == 0, hc == KH - 1), incs)
                xnsem = "xn2A" if p % 2 == 0 else "xn2B"
                for d in range(8):
                    i3 = 8 * p + d
                    dc = dg * 8 + d
                    if i3 >= 4:
                        P.wait("sync", "outD", 16 * (i3 - 3))
                    P.dma("sync", xwh(i3), d_spxn[dc * 128:(dc + 1) * 128, th * TH:(th + 1) * TH],
                          xnsem)
                for _f in range(3):
                    P.dma("sync", junk[:, 0:8], d_spxn[0:128, 0:8], xnsem)
                for d in range(8):
                    i3 = 8 * p + d
                    dc = dg * 8 + d
                    P.wait("vector", "mmp", KH * (p + 1))
                    P.wait("vector", xnsem, (p // 2) * 11 * 16 + 16 * min(d + 4, 11))
                    P.op("vector", STT(scrh(i3 % 2), pm8[d][:, :], ALU.mult, gcol[:, 10:11],
                                       ALU.mult, bc1[:, th * TH:(th + 1) * TH]), [("evP", 1)])
                    if i3 >= 2:
                        P.wait("vector", "outD", 16 * (i3 - 1))
                    P.op("vector", STT(scrh(2 + i3 % 2), scrh(i3 % 2), ALU.add, boc[:, dc:dc + 1],
                                       ALU.add, xwh(i3)), [("fo", 1)])
                    P.wait("gpsimd", "fo", P.n("fo"))
                    P.op("gpsimd", lambda g, dc=dc, th=th, src=scrh(2 + i3 % 2): g.dma_start(
                        out=d_out[dc * 128:(dc + 1) * 128, th * TH:(th + 1) * TH], in_=src),
                        [("outD", 16)])

            if d_dbg:
                P.wait("gpsimd", "outD", P.n("outD"))
                if "ln1" in d_dbg:
                    pass
                if "a" in d_dbg:
                    P.dma("gpsimd", d_dbg["a"][:, :], A[:, :], "dbgd")
                if "gam" in d_dbg:
                    P.dma("gpsimd", d_dbg["gam"][:, :], gcol[:, :], "dbgd")
                if "row" in d_dbg:
                    P.dma("gpsimd", d_dbg["row"][:, :], trow[:, :], "dbgd")
                if "xn" in d_dbg:
                    P.dma("gpsimd", d_dbg["xn"][:, :], d_spxn[:, :], "dbgd")
                if "hp" in d_dbg:
                    P.dma("gpsimd", d_dbg["hp"][:, :], d_sph[:, :], "dbgd")
                if "xdq" in d_dbg:
                    P.dma("gpsimd", d_dbg["xdq"][:, :], actB[:, :].bitcast(BF16) if False
                          else actB[:, 0:KD * T], "dbgd")
                P.wait("gpsimd", "dbgd", P.n("dbgd"))

            P.wait("sync", "outD", 32 * 16)
            P.wait("gpsimd", "outD", 32 * 16)


        except _StopEmit:
            P.dma("gpsimd", d_out[0:128, 0:T], actA[:, 0:T], "outD")
            P.wait("gpsimd", "outD", P.n("outD"))
            P.wait("sync", "outD", P.n("outD"))

        sems = {}
        for g in ENGS:
            for item in P.ops[g]:
                nm = item[1] if item[0] == "w" else None
                if nm and nm not in sems:
                    sems[nm] = es.enter_context(nc.semaphore("sem_" + nm))
                if item[0] == "o":
                    for s, _k in item[2]:
                        if s not in sems:
                            sems[s] = es.enter_context(nc.semaphore("sem_" + s))

        block = e(nc.Block())

        def replay(eng_name):
            def fn(handle):
                for item in P.ops[eng_name]:
                    if item[0] == "w":
                        handle.wait_ge(sems[item[1]], item[2])
                    else:
                        instr = item[1](handle)
                        for s, k in item[2]:
                            instr.then_inc(sems[s], k)
            return fn

        block.sync(replay("sync"))
        block.gpsimd(replay("gpsimd"))
        block.vector(replay("vector"))
        block.scalar(replay("scalar"))
        block.tensor(replay("tensor"))

    return nc


def _prep(inputs):
    X = np.asarray(inputs["L0_attn_X"], dtype=np.float32)
    Win = np.asarray(inputs["L0_attn_W_in"], dtype=np.float32)
    bin_ = np.asarray(inputs["L0_attn_b_in"], dtype=np.float32)
    Wout = np.asarray(inputs["L0_attn_W_out"], dtype=np.float32)
    bo_ = np.asarray(inputs["L0_attn_b_out"], dtype=np.float32)
    gw = np.asarray(inputs["L0_ffn_gate_w"], dtype=np.float32)
    gb = np.asarray(inputs["L0_ffn_gate_b"], dtype=np.float32)
    vw = np.asarray(inputs["L0_ffn_val_w"], dtype=np.float32)
    vb = np.asarray(inputs["L0_ffn_val_b"], dtype=np.float32)
    ow = np.asarray(inputs["L0_ffn_out_w"], dtype=np.float32)
    ob = np.asarray(inputs["L0_ffn_out_b"], dtype=np.float32)

    perm = np.concatenate([np.arange(0, 1024, 2), np.arange(1, 1024, 2),
                           np.arange(1024, 2048)])
    rows = np.concatenate([perm, 2048 + perm, 4096 + perm])
    wqkvT = np.ascontiguousarray(Win[rows, :].T)
    bq = bin_[rows]
    bqkv_c = np.ascontiguousarray(bq.reshape(KE, 128).T)
    woutT = np.ascontiguousarray(Wout.T[perm, :])
    bout_c = np.ascontiguousarray(bo_.reshape(KD, 128).T)
    gwT = np.ascontiguousarray(gw.T)
    vwT = np.ascontiguousarray(vw.T)
    owT = np.ascontiguousarray(ow.T)
    bg_c = np.ascontiguousarray(gb.reshape(KH, 128).T)
    bv_c = np.ascontiguousarray(vb.reshape(KH, 128).T)
    bo_c = np.ascontiguousarray(ob.reshape(KD, 128).T)
    ident = np.eye(128, dtype=np.float32)
    onesr = np.ones((1, 128), dtype=np.float32)

    inv = 10000.0 ** (-np.arange(0, 1024, 2, dtype=np.float64) / 1024.0)
    Xf = X.reshape(B * S, D)

    shared = dict(wqkvT=wqkvT, woutT=woutT, bqkv=bqkv_c, bout=bout_c,
                  bg=bg_c, bv=bv_c, bo=bo_c, ident=ident, onesr=onesr)
    per_core = []
    for c in range(NCORES):
        s0 = (c % 4) * T
        ang = np.arange(s0, s0 + T, dtype=np.float64)[None, :] * inv[:, None]
        cosT = np.cos(ang).astype(np.float32)
        sinT = np.sin(ang).astype(np.float32)
        xT = np.ascontiguousarray(Xf[c * T:(c + 1) * T, :].T)
        gslc = np.ascontiguousarray(gwT[c * (D // 8):(c + 1) * (D // 8), :])
        vslc = np.ascontiguousarray(vwT[c * (D // 8):(c + 1) * (D // 8), :])
        oslc = np.ascontiguousarray(owT[c * (H // 8):(c + 1) * (H // 8), :])
        m = dict(shared)
        m.update(xT=xT, cosT=cosT, sinT=sinT, gw_slc=gslc, vw_slc=vslc, ow_slc=oslc)
        per_core.append(m)
    return per_core


def kernel(**inputs):
    in_maps = _prep(inputs)
    nc = build()
    res = run_bass_kernel_spmd(nc, in_maps, core_ids=list(range(NCORES)))
    out = np.concatenate([np.asarray(r["outT"]).T for r in res.results], axis=0)
    return np.ascontiguousarray(out.reshape(B, S, D).astype(np.float32))
